# version 01
# baseline (speedup 1.0000x reference)
import math
import contextlib
import numpy as np
import concourse.bass as bass
import concourse.mybir as mybir
from concourse.bass_utils import run_bass_kernel_spmd

F32 = mybir.dt.float32
BF16 = mybir.dt.bfloat16
AF = mybir.ActivationFunctionType
ALU = mybir.AluOpType

NCORES = 8
S = 1024
D = 2048
DI = 4096
HF = 8192
NST = 16
SCALE = 128.0 ** -0.5
MSHIFT = 12.0
EPS = 1e-5
RS8192 = 1.0 / math.sqrt(8192.0)

_UPTO = 99
_DBG = None
_STOP_LNX = False


class Ctx:
    def __init__(self, nc, stack):
        self.nc = nc
        self.stack = stack
        names = ["dw", "ds", "dx", "cc", "mm", "vv", "ss", "gg"]
        self.sem = {n: stack.enter_context(nc.semaphore(n)) for n in names}
        self.cnt = {n: 0 for n in names}

    def new_sem(self, name):
        if name not in self.sem:
            self.sem[name] = self.stack.enter_context(self.nc.semaphore(name))
            self.cnt[name] = 0
        return name

    def inc(self, instr, n, v=1):
        self.cnt[n] += v
        instr.then_inc(self.sem[n], v)
        return self.cnt[n]

    def dinc(self, instr, n):
        return self.inc(instr, n, 16)

    def cinc(self, instr, n):
        self.cnt[n] += 1
        instr.then_inc(self.sem[n])
        return self.cnt[n]

    def wait(self, eng, n, at=None):
        v = self.cnt[n] if at is None else at
        if v > 0:
            eng.wait_ge(self.sem[n], v)


def build_program(upto=99, dbg=None):
    nc = bass.Bass("TRN2", target_bir_lowering=False, debug=False)

    def din(name, shape, dtype=F32):
        return nc.dram_tensor(name, shape, dtype, kind="ExternalInput").ap()

    x_slab = din("x_slab", [D, S])
    wq_p = din("wq_p", [D, D])
    wk_p = din("wk_p", [512, D])
    wv_h = din("wv_h", [D, 512])
    wdense_p = din("wdense_p", [D, D])
    winp_p = din("winp_p", [HF, D])
    wxp_p = din("wxp_p", [256, DI])
    wdtp_p = din("wdtp_p", [DI, 128])
    woutp_p = din("woutp_p", [D, DI])
    wffp_p = din("wffp_p", [HF, D])
    wffo_p = din("wffo_p", [D, DI])
    w1_p = din("w1_p", [HF, D])
    w3_p = din("w3_p", [HF, D])
    w2t = din("w2t", [HF, 256])
    conv_w = din("conv_w", [128, 32 * 4])
    conv_b = din("conv_b", [128, 32])
    dt_b = din("dt_b", [128, 32])
    a_log = din("a_log", [128, 32 * NST])
    d_skip = din("d_skip", [128, 32])
    ln1_w = din("ln1_w", [128, 16])
    ln1_b = din("ln1_b", [128, 16])
    ln2_w = din("ln2_w", [128, 16])
    ln2_b = din("ln2_b", [128, 16])
    h128_in = din("h128", [128, 128])
    kidx_in = din("kidx", [128, 64])
    qidx_in = din("qidx", [128, S])
    sel_edge = din("sel_edge", [8, 1])
    sel_carry = din("sel_carry", [128, 8])

    y_out = nc.dram_tensor("y", [D, S], F32, kind="ExternalOutput").ap()
    dbg_out = nc.dram_tensor("dbg", [4096, S], F32, kind="ExternalOutput").ap()

    def dint(name, shape, dtype, shared=False):
        return nc.dram_tensor(name, shape, dtype,
                              addr_space="Shared" if shared else "Local").ap()

    xres = dint("xres", [D, S], F32)
    lx_dram = dint("lx_dram", [D, S], F32)
    amid_dram = dint("amid_dram", [D, S], F32)
    kag_in = dint("kag_in", [512, S], BF16)
    kag_out = dint("kag_out", [8 * 512, S], BF16, shared=True)
    vag_in = dint("vag_in", [S, 512], BF16)
    vag_out = dint("vag_out", [8 * S, 512], BF16, shared=True)
    w2h_in = dint("w2h_in", [256, HF], BF16)
    w2h_out = dint("w2h_out", [8 * 256, HF], BF16, shared=True)
    edge_in = dint("edge_in", [1, DI * 3], BF16)
    edge_out = dint("edge_out", [8, DI * 3], BF16, shared=True)
    sum_in = dint("sum_in", [128, 32 * 2 * NST], F32)
    sum_out = dint("sum_out", [8 * 128, 32 * 2 * NST], F32, shared=True)
    sz_dram = dint("sz_dram", [DI, S], BF16)
    dxc_dram = dint("dxc_dram", [DI, S], BF16)
    g_dram = dint("g_dram", [DI, S], BF16)
    y_dram = dint("y_dram", [DI, S], F32)
    bc_dram = dint("bc_dram", [32, S], BF16)
    l_dram = dint("l_dram", [128, 512], F32)

    stack = contextlib.ExitStack()
    with stack:
        block = stack.enter_context(nc.Block())
        ctx = Ctx(nc, stack)

        def sbuf(name, shape, dtype, ph=None):
            tgt = ph if ph is not None else stack
            return tgt.enter_context(nc.sbuf_tensor(name, shape, dtype))

        def psum(name, shape, ph=None):
            tgt = ph if ph is not None else stack
            return tgt.enter_context(nc.psum_tensor(name, shape, F32))

        ones1 = sbuf("ones1", [1, 128], BF16)
        onesc = sbuf("onesc", [128, 1], BF16)
        h128_sb = sbuf("h128_sb", [128, 128], BF16)
        kidx_sb = sbuf("kidx_sb", [128, 64], F32)
        convw_sb = sbuf("convw_sb", [128, 32 * 4], F32)
        convb_sb = sbuf("convb_sb", [128, 32], F32)
        dtb_sb = sbuf("dtb_sb", [128, 32], F32)
        a_sb = sbuf("a_sb", [128, 32 * NST], F32)
        dsk_sb = sbuf("dsk_sb", [128, 32], F32)
        ln1w_sb = sbuf("ln1w_sb", [128, 16], F32)
        ln1b_sb = sbuf("ln1b_sb", [128, 16], F32)
        ln2w_sb = sbuf("ln2w_sb", [128, 16], F32)
        ln2b_sb = sbuf("ln2b_sb", [128, 16], F32)
        sele_sb = sbuf("sele_sb", [8, 1], BF16)
        selc_sb = sbuf("selc_sb", [128, 8], F32)
        xin_st = sbuf("xin_st", [128, 2, 512], F32)
        xmid_st = sbuf("xmid_st", [128, 2, 512], F32)
        xout_st = sbuf("xout_st", [128, 2, 512], F32)

        cbias = sbuf("cbias", [128, 3], F32)

        @block.gpsimd
        def _(g):
            g.memset(ones1[:], 1.0)
            g.memset(onesc[:], 1.0)
            g.memset(cbias[:, 0:1], EPS)
            g.memset(cbias[:, 1:2], 1.0)
            ctx.inc(g.memset(cbias[:, 2:3], -MSHIFT), "gg")
            for dst, src in [(h128_sb, h128_in), (kidx_sb, kidx_in),
                             (convw_sb, conv_w),
                             (convb_sb, conv_b), (dtb_sb, dt_b),
                             (a_sb, a_log), (dsk_sb, d_skip),
                             (ln1w_sb, ln1_w), (ln1b_sb, ln1_b),
                             (ln2w_sb, ln2_w), (ln2b_sb, ln2_b),
                             (sele_sb, sel_edge), (selc_sb, sel_carry)]:
                ctx.dinc(g.dma_start(out=dst[:], in_=src[:, :]), "dw")
        p0_dw = ctx.cnt["dw"]
        ctx.p0dw = p0_dw

        @block.scalar
        def _(s):
            ctx.wait(s, "dw", p0_dw)
            ctx.inc(s.activation(out=a_sb[:], in_=a_sb[:], func=AF.Exp), "ss")
        a_exp = ctx.cnt["ss"]

        @block.vector
        def _(v):
            ctx.wait(v, "ss", a_exp)
            ctx.inc(v.tensor_single_scalar(a_sb[:], a_sb[:], -1.0, ALU.mult), "vv")

        with contextlib.ExitStack() as ph:
            wa = sbuf("w2s_a", [128, 64 * 256], F32, ph)
            wb = sbuf("w2s_b", [128, 64 * 256], F32, ph)
            wbf = sbuf("w2s_bf", [128, 64 * 256], BF16, ph)

            w2tsm = ctx.new_sem("w2tld")

            @block.gpsimd
            def _(g):
                ctx.inc(g.dma_start(out=wa[:, :], in_=w2t[:, :]), w2tsm, 16)
            w2_ld = ctx.cnt[w2tsm]

            @block.vector
            def _(v):
                ctx.wait(v, "w2tld", w2_ld)
                src, dst = wa, wb
                for hh in [1, 2, 4, 8, 16, 32]:
                    sv = src.ap().rearrange("p (g two h d) -> p g two h d",
                                            two=2, h=hh, d=256)
                    dv = dst.ap().rearrange("p (g two h d) -> p g two h d",
                                            two=2, h=hh, d=256)
                    v.tensor_tensor(dv[:, :, 0, :, :], sv[:, :, 0, :, :],
                                    sv[:, :, 1, :, :], ALU.add)
                    v.tensor_tensor(dv[:, :, 1, :, :], sv[:, :, 0, :, :],
                                    sv[:, :, 1, :, :], ALU.subtract)
                    src, dst = dst, src
                ctx.inc(v.tensor_single_scalar(wbf[:], src[:], RS8192, ALU.mult),
                        "vv")
            w2h_cast = ctx.cnt["vv"]

            @block.gpsimd
            def _(g):
                ctx.wait(g, "vv", w2h_cast)
                wv3 = wbf.ap().rearrange("p (i d) -> p i d", i=64)
                for dh in range(2):
                    ctx.dinc(g.dma_start(
                        out=w2h_in[dh * 128:(dh + 1) * 128, :].rearrange(
                            "p (a d) -> p a d", d=128),
                        in_=wv3[:, :, dh * 128:(dh + 1) * 128]), "dw")
                ctx.wait(g, "dw")
                ctx.cinc(g.collective_compute(
                    "AllGather", ALU.bypass,
                    replica_groups=[list(range(NCORES))],
                    ins=[w2h_in.opt()], outs=[w2h_out.opt()]), "cc")
        w2h_ready = ctx.cnt["cc"]

        def ln_phase(src_dram, lnx, wcol=None, bcol=None, tag="",
                     f32_dst=None):
            @block.sync
            def _(sy):
                ctx.wait(sy, "ds")
                ctx.wait(sy, "vv")
                ctx.wait(sy, "ss")
                ctx.wait(sy, "mm")
                ctx.wait(sy, "dw")
            with contextlib.ExitStack() as ph:
                xt = sbuf(f"lnxt{tag}", [128, 2, 512], F32, ph)
                xb = sbuf(f"lnxb{tag}", [128, 2, 512], BF16, ph)
                xq = sbuf(f"lnxq{tag}", [128, 2, 512], BF16, ph)
                row = sbuf(f"lnrow{tag}", [1, 3 * 512], F32, ph)
                rbf = sbuf(f"lnrbf{tag}", [1, 2 * 512], BF16, ph)
                ps_m = psum(f"lnpm{tag}", [1, 512], ph)
                ps_q = psum(f"lnpq{tag}", [1, 512], ph)
                ps_mr = psum(f"lnpmr{tag}", [128, 512], ph)
                ps_rr = psum(f"lnprr{tag}", [128, 512], ph)
                for qc in range(2):
                    qs = slice(qc * 512, qc * 512 + 512)
                    cast_v, sq_v, stats_mm = [], [], []
                    for ct in range(16):
                        sl = ct % 2

                        lsm = ctx.new_sem(f"ln{sl}")

                        @block.sync
                        def _(sy, ct=ct, sl=sl, qc=qc, lsm=lsm):
                            if ct >= 2:
                                ctx.wait(sy, "vv", cast_v[ct - 2])
                            elif qc > 0:
                                ctx.wait(sy, "vv")
                            ctx.inc(sy.dma_start(
                                out=xt[:, sl, :],
                                in_=src_dram[ct * 128:(ct + 1) * 128, qs]),
                                lsm, 16)
                        ld = (lsm, ctx.cnt[lsm])

                        @block.vector
                        def _(v, ld=ld, sl=sl, ct=ct, qc=qc):
                            ctx.wait(v, ld[0], ld[1])
                            if ct >= 2:
                                ctx.wait(v, "ss", sq_v[ct - 2])
                                ctx.wait(v, "mm", stats_mm[ct - 2])
                            elif qc > 0:
                                ctx.wait(v, "mm")
                            ctx.inc(v.tensor_copy(xb[:, sl, :], xt[:, sl, :]),
                                    "vv")
                        cast_v.append(ctx.cnt["vv"])

                        @block.scalar
                        def _(s, sl=sl, ct=ct):
                            ctx.wait(s, "vv", cast_v[-1])
                            if ct >= 2:
                                ctx.wait(s, "mm", stats_mm[ct - 2])
                            ctx.inc(s.activation(out=xq[:, sl, :],
                                                 in_=xb[:, sl, :],
                                                 func=AF.Square), "ss")
                        sq_v.append(ctx.cnt["ss"])

                        @block.tensor
                        def _(t, ct=ct, sl=sl, qc=qc, c=cast_v[-1], q=sq_v[-1]):
                            if ct == 0:
                                ctx.wait(t, "gg", 1)
                            if ct == 0 and qc > 0:
                                ctx.wait(t, "vv")
                            ctx.wait(t, "vv", c)
                            t.matmul(ps_m[0:1, :], onesc[:], xb[:, sl, :],
                                     start=(ct == 0), stop=(ct == 15))
                            ctx.wait(t, "ss", q)
                            i = t.matmul(ps_q[0:1, :], onesc[:], xq[:, sl, :],
                                         start=(ct == 0), stop=(ct == 15))
                            ctx.inc(i, "mm")
                        stats_mm.append(ctx.cnt["mm"])
                    stats = ctx.cnt["mm"]

                    @block.vector
                    def _(v, stats=stats):
                        ctx.wait(v, "mm", stats)
                        v.tensor_single_scalar(row[0:1, 0:512], ps_m[0:1, :],
                                               1.0 / D, ALU.mult)
                        v.tensor_single_scalar(row[0:1, 512:1024], ps_q[0:1, :],
                                               1.0 / D, ALU.mult)
                        v.tensor_mul(row[0:1, 1024:1536], row[0:1, 0:512],
                                     row[0:1, 0:512])
                        ctx.inc(v.tensor_sub(row[0:1, 512:1024],
                                             row[0:1, 512:1024],
                                             row[0:1, 1024:1536]), "vv")
                    var_done = ctx.cnt["vv"]

                    @block.scalar
                    def _(s, var_done=var_done):
                        ctx.wait(s, "vv", var_done)
                        ctx.wait(s, "gg", 1)
                        s.activation(out=row[0:1, 1024:1536],
                                     in_=row[0:1, 512:1024], func=AF.Ln,
                                     bias=cbias[0:1, 0:1])
                        ctx.inc(s.activation(out=row[0:1, 512:1024],
                                             in_=row[0:1, 1024:1536],
                                             func=AF.Exp, scale=-0.5), "ss")
                    rstd = ctx.cnt["ss"]

                    @block.vector
                    def _(v, rstd=rstd):
                        ctx.wait(v, "ss", rstd)
                        ctx.wait(v, "mm")
                        v.tensor_copy(rbf[0:1, 0:512], row[0:1, 0:512])
                        ctx.inc(v.tensor_copy(rbf[0:1, 512:1024],
                                              row[0:1, 512:1024]), "vv")
                    rows_bf = ctx.cnt["vv"]

                    @block.tensor
                    def _(t, rows_bf=rows_bf):
                        ctx.wait(t, "vv", rows_bf)
                        t.matmul(ps_mr[:], ones1[0:1, :], rbf[0:1, 0:512],
                                 start=True, stop=True)
                        ctx.inc(t.matmul(ps_rr[:], ones1[0:1, :],
                                         rbf[0:1, 512:1024],
                                         start=True, stop=True), "mm")
                    reps = ctx.cnt["mm"]

                    app_v, st_d = [], []
                    for ct in range(16):
                        sl = ct % 2

                        lsm = ctx.new_sem(f"ln{sl}")

                        @block.sync
                        def _(sy, ct=ct, sl=sl, qc=qc, lsm=lsm):
                            if ct >= 2:
                                ctx.wait(sy, "vv", app_v[ct - 2])
                            else:
                                ctx.wait(sy, "vv")
                            ctx.inc(sy.dma_start(
                                out=xt[:, sl, :],
                                in_=src_dram[ct * 128:(ct + 1) * 128, qs]),
                                lsm, 16)
                        ld2 = (lsm, ctx.cnt[lsm])

                        @block.vector
                        def _(v, ld2=ld2, ct=ct, sl=sl, reps=reps):
                            ctx.wait(v, ld2[0], ld2[1])
                            ctx.wait(v, "mm", reps)
                            if ct == 0:
                                ctx.wait(v, "dw", ctx.p0dw)
                            if f32_dst is not None and ct >= 2:
                                ctx.wait(v, st_d[ct - 2][0], st_d[ct - 2][1])
                            v.tensor_sub(xt[:, sl, :], xt[:, sl, :], ps_mr[:])
                            v.tensor_tensor(xt[:, sl, :], xt[:, sl, :],
                                            ps_rr[:], ALU.mult)
                            if wcol is None:
                                ctx.inc(v.tensor_copy(lnx[:, ct, qs],
                                                      xt[:, sl, :]), "vv")
                            else:
                                ctx.inc(v.tensor_scalar(
                                    lnx[:, ct, qs], xt[:, sl, :],
                                    wcol[:, ct:ct + 1], bcol[:, ct:ct + 1],
                                    ALU.mult, ALU.add), "vv")
                        app_v.append(ctx.cnt["vv"])

                        if f32_dst is not None:
                            xsm = ctx.new_sem(f"lx{sl}")

                            @block.sync
                            def _(sy, ct=ct, sl=sl, xsm=xsm):
                                ctx.wait(sy, "vv", app_v[-1])
                                ctx.inc(sy.dma_start(
                                    out=f32_dst[ct * 128:(ct + 1) * 128, qs],
                                    in_=xt[:, sl, :]), xsm, 16)
                            st_d.append((xsm, ctx.cnt[xsm]))
            return ctx.cnt["vv"]

        class WStream:

            def __init__(self, name, ph, dram, n_ct, slots=3):
                self.dram = dram
                self.n_ct = n_ct
                self.slots = slots
                self.t = sbuf(name, [128, slots, n_ct * 128], BF16, ph)
                self.loaded = {}
                self.sems = [ctx.new_sem(f"{name}_s{k}") for k in range(slots)]

            def tile(self, gi, ct):
                v = self.t.ap().rearrange("p sl (ct o) -> p sl ct o", o=128)
                return v[:, gi % self.slots, ct, :]

            def load(self, eng, gi, release=None, release_sem="vv"):
                if release is not None and release > 0:
                    eng.wait_ge(ctx.sem[release_sem], release)
                i = eng.dma_start(out=self.t[:, gi % self.slots, :],
                                  in_=self.dram[gi * 128:(gi + 1) * 128, :])
                sm = self.sems[gi % self.slots]
                self.loaded[gi] = (sm, ctx.inc(i, sm, 16))

        def res_evac(st, ot, qc, pp, mm_at, srcs, dst):
            idx = st["idx"]
            st["idx"] += 1
            sl = idx % 2
            rl = ctx.new_sem(f"rl{sl}")
            rs = ctx.new_sem(f"rs{sl}")

            @block.sync
            def _(sy, ot=ot, qc=qc, sl=sl, rl=rl):
                if idx >= 2:
                    ctx.wait(sy, "vv", st["add"][idx - 2])
                else:
                    ctx.wait(sy, "ds")
                    ctx.wait(sy, "vv")
                ctx.inc(sy.dma_start(
                    out=xin_st[:, sl, :],
                    in_=srcs[0][ot * 128:(ot + 1) * 128,
                                qc * 512:(qc + 1) * 512]), rl, 16)
                if len(srcs) > 1:
                    ctx.inc(sy.dma_start(
                        out=xmid_st[:, sl, :],
                        in_=srcs[1][ot * 128:(ot + 1) * 128,
                                    qc * 512:(qc + 1) * 512]), rl, 16)
            ld = (rl, ctx.cnt[rl])

            @block.vector
            def _(v, sl=sl, ld=ld, mm_at=mm_at, rs=rs):
                ctx.wait(v, ld[0], ld[1])
                ctx.wait(v, "mm", mm_at)
                if idx >= 2:
                    ctx.wait(v, st["store"][idx - 2][0],
                             st["store"][idx - 2][1])
                if len(srcs) > 1:
                    v.tensor_tensor(xin_st[:, sl, :], xin_st[:, sl, :],
                                    xmid_st[:, sl, :], ALU.add)
                ctx.inc(v.tensor_tensor(xout_st[:, sl, :], xin_st[:, sl, :],
                                        pp[:], ALU.add), "vv")
            st["add"].append(ctx.cnt["vv"])

            @block.sync
            def _(sy, ot=ot, qc=qc, sl=sl, rs=rs):
                ctx.wait(sy, "vv", st["add"][idx])
                ctx.inc(sy.dma_start(
                    out=dst[ot * 128:(ot + 1) * 128, qc * 512:(qc + 1) * 512],
                    in_=xout_st[:, sl, :]), rs, 16)
            st["store"].append((rs, ctx.cnt[rs]))
            return st["add"][idx]

        def new_evac_state():
            return {"idx": 0, "add": [], "store": []}

        def phase_attn():
            with contextlib.ExitStack() as ph:
                ctx_acc = sbuf("p1_ctx", [128, 16, S], F32, ph)
                phQ = ph.enter_context(contextlib.ExitStack())
                qsb = sbuf("p1_q", [128, 16, S], BF16, phQ)
                with contextlib.ExitStack() as ph2:
                    lnx = sbuf("p1_lnx", [128, 16, S], BF16, ph2)
                    ln_phase(x_slab, lnx, tag="0")
                    lnx_done = ctx.cnt["vv"]

                    if dbg == "lnx":
                        @block.gpsimd
                        def _(g):
                            ctx.wait(g, "vv", lnx_done)
                            for ct in range(16):
                                ctx.dinc(g.dma_start(
                                    out=dbg_out[ct * 128:(ct + 1) * 128, :],
                                    in_=lnx[:, ct, :]), "dw")
                            ctx.wait(g, "dw")

                        @block.vector
                        def _(v):
                            ctx.wait(v, "dw")
                    if dbg == "lnx" and _STOP_LNX:
                        return

                    psA = psum("p1_psA", [128, 512], ph2)
                    psB = psum("p1_psB", [128, 512], ph2)
                    psC = psum("p1_psC", [128, 512], ph2)
                    psD = psum("p1_psD", [128, 512], ph2)
                    ws_q = WStream("p1_wsq", ph2, wq_p, 16)
                    ws_k = WStream("p1_wsk", ph2, wk_p, 16)
                    kvst = sbuf("p1_kvst", [128, 2, 512], BF16, ph2)
                    wv_sb = sbuf("p1_wv", [128, 16, 512], BF16, ph2)

                    evac_q = []
                    for ot in range(16):
                        @block.gpsimd
                        def _(g, ot=ot):
                            rel = evac_q[ot - 3] if ot >= 3 else lnx_done
                            ws_q.load(g, ot, rel)
                        ld = ws_q.loaded[ot]

                        @block.tensor
                        def _(t, ot=ot, ld=ld):
                            ctx.wait(t, ld[0], ld[1])
                            ctx.wait(t, "vv", lnx_done)
                            if ot >= 2:
                                ctx.wait(t, "vv", evac_q[ot - 2])
                            for qc in range(2):
                                pp = ([psA, psB] if ot % 2 == 0
                                      else [psC, psD])[qc]
                                for ct in range(16):
                                    i = t.matmul(
                                        pp[:], ws_q.tile(ot, ct),
                                        lnx[:, ct, qc * 512:(qc + 1) * 512],
                                        start=(ct == 0), stop=(ct == 15))
                                ctx.inc(i, "mm")
                        mmv = ctx.cnt["mm"]

                        @block.vector
                        def _(v, ot=ot, mmv=mmv):
                            pps = [psA, psB] if ot % 2 == 0 else [psC, psD]
                            ctx.wait(v, "mm", mmv - 1)
                            v.tensor_copy(qsb[:, ot, 0:512], pps[0][:])
                            ctx.wait(v, "mm", mmv)
                            ctx.inc(v.tensor_copy(qsb[:, ot, 512:1024],
                                                  pps[1][:]), "vv")
                        evac_q.append(ctx.cnt["vv"])

                    evac_k, store_k = [], []
                    for ot in range(4):
                        @block.gpsimd
                        def _(g, ot=ot):
                            rel = evac_k[2 * (ot - 2) + 1] if ot >= 2 else lnx_done
                            ws_k.load(g, ot, rel)
                        ld = ws_k.loaded[ot]

                        @block.tensor
                        def _(t, ot=ot, ld=ld):
                            ctx.wait(t, ld[0], ld[1])
                            ctx.wait(t, "vv", lnx_done)
                            if ot >= 1:
                                ctx.wait(t, "vv", evac_k[2 * (ot - 1) + 1])
                            else:
                                ctx.wait(t, "vv")
                            for qc in range(2):
                                pp = [psA, psB][qc]
                                for ct in range(16):
                                    i = t.matmul(
                                        pp[:], ws_k.tile(ot, ct),
                                        lnx[:, ct, qc * 512:(qc + 1) * 512],
                                        start=(ct == 0), stop=(ct == 15))
                                ctx.inc(i, "mm")
                        mmv = ctx.cnt["mm"]

                        for qc in range(2):
                            ksm = ctx.new_sem(f"kv{qc}")

                            @block.vector
                            def _(v, mmv=mmv, qc=qc, ot=ot, ksm=ksm):
                                ctx.wait(v, "mm", mmv - 1 + qc)
                                if len(store_k) >= 2:
                                    ctx.wait(v, store_k[-2][0], store_k[-2][1])
                                ctx.inc(v.tensor_copy(kvst[:, qc, :],
                                                      [psA, psB][qc][:]), "vv")
                            evac_k.append(ctx.cnt["vv"])

                            @block.sync
                            def _(sy, ot=ot, qc=qc, ksm=ksm):
                                ctx.wait(sy, "vv", evac_k[-1])
                                ctx.inc(sy.dma_start(
                                    out=kag_in[ot * 128:(ot + 1) * 128,
                                               qc * 512:(qc + 1) * 512],
                                    in_=kvst[:, qc, :]), ksm, 16)
                            store_k.append((ksm, ctx.cnt[ksm]))

                    @block.gpsimd
                    def _(g):
                        ctx.wait(g, "vv", lnx_done)
                        ctx.dinc(g.dma_start(
                            out=wv_sb[:, :, :],
                            in_=wv_h.rearrange("(ct p) o -> p ct o", p=128)),
                            "dw")
                    wv_ld = ctx.cnt["dw"]

                    evac_v, store_v = [], []
                    for st in range(8):
                        @block.tensor
                        def _(t, st=st):
                            ctx.wait(t, "dw", wv_ld)
                            ctx.wait(t, "vv", lnx_done)
                            if st >= 2:
                                ctx.wait(t, "vv", evac_v[st - 2])
                            else:
                                ctx.wait(t, "vv")
                            pp = psA if st % 2 == 0 else psB
                            for ct in range(16):
                                i = t.matmul(pp[:],
                                             lnx[:, ct, st * 128:(st + 1) * 128],
                                             wv_sb[:, ct, :],
                                             start=(ct == 0), stop=(ct == 15))
                            ctx.inc(i, "mm")
                        mmv = ctx.cnt["mm"]

                        vsm = ctx.new_sem(f"kv{st % 2}")

                        @block.vector
                        def _(v, st=st, mmv=mmv):
                            ctx.wait(v, "mm", mmv)
                            if st >= 2:
                                ctx.wait(v, store_v[st - 2][0],
                                         store_v[st - 2][1])
                            pp = psA if st % 2 == 0 else psB
                            ctx.inc(v.tensor_copy(kvst[:, st % 2, :], pp[:]),
                                    "vv")
                        evac_v.append(ctx.cnt["vv"])

                        @block.sync
                        def _(sy, st=st, vsm=vsm):
                            ctx.wait(sy, "vv", evac_v[-1])
                            ctx.inc(sy.dma_start(
                                out=vag_in[st * 128:(st + 1) * 128, :],
                                in_=kvst[:, st % 2, :]), vsm, 16)
                        store_v.append((vsm, ctx.cnt[vsm]))
                    v_kv0 = ctx.cnt["kv0"]
                    v_kv1 = ctx.cnt["kv1"]

                    @block.gpsimd
                    def _(g):
                        ctx.wait(g, "kv0", v_kv0)
                        ctx.wait(g, "kv1", v_kv1)
                        ctx.cinc(g.collective_compute(
                            "AllGather", ALU.bypass,
                            replica_groups=[list(range(NCORES))],
                            ins=[kag_in.opt()], outs=[kag_out.opt()]), "cc")
                        ctx.cinc(g.collective_compute(
                            "AllGather", ALU.bypass,
                            replica_groups=[list(range(NCORES))],
                            ins=[vag_in.opt()], outs=[vag_out.opt()]), "cc")
                kv_ready = ctx.cnt["cc"]

                qidx_sb = sbuf("p1_qidx", [128, S], F32, phQ)
                qsm = ctx.new_sem("qidxld")

                @block.gpsimd
                def _(g):
                    ctx.wait(g, "vv", lnx_done)
                    ctx.inc(g.dma_start(out=qidx_sb[:], in_=qidx_in[:, :]),
                            qsm, 16)
                qidx_ld = ctx.cnt[qsm]
                lrow = sbuf("p1_lrow", [1, 2, 512], F32, phQ)
                masks = sbuf("p1_msk", [128, 32, 512], BF16, phQ)
                kst = sbuf("p1_kst", [128, 2, 16 * 128], BF16, phQ)
                vst = sbuf("p1_vst", [128, 2, 16 * 128], BF16, phQ)
                punm = sbuf("p1_punm", [128, 2, 512], BF16, phQ)
                pmsk = sbuf("p1_pm", [128, 16, 512], BF16, phQ)
                ps_s0 = psum("p1_ps0", [128, 512], phQ)
                ps_s1 = psum("p1_ps1", [128, 512], phQ)
                ps_c0 = psum("p1_pc0", [128, 512], phQ)
                ps_c1 = psum("p1_pc1", [128, 512], phQ)
                ps_l0 = psum("p1_pl0", [1, 512], phQ)
                ps_l1 = psum("p1_pl1", [1, 512], phQ)

                iters = []
                for qu in range(4):
                    for gi in range(4):
                        for hh in range(4):
                            for qc in range(2):
                                iters.append((qu, gi, gi * 4 + hh, qc))

                sT_mm, ctxl_mm, exp_ss, stt_vv, evac_vv, mask_vv = \
                    {}, {}, {}, {}, {}, {}
                mm_c, ss_c, vv_c = ctx.cnt["mm"], ctx.cnt["ss"], ctx.cnt["vv"]
                for j, (qu, gi, hh, qc) in enumerate(iters):
                    if j % 32 == 0:
                        vv_c += 1
                        mask_vv[qu] = vv_c
                    for kt in range(16):
                        mm_c += 1
                        sT_mm[(j, kt)] = mm_c
                        ss_c += 1
                        exp_ss[(j, kt)] = ss_c
                        vv_c += 1
                        stt_vv[(j, kt)] = vv_c
                    mm_c += 1
                    ctxl_mm[j] = mm_c
                    vv_c += 2
                    evac_vv[j] = vv_c

                lrow_ds = []
                for qu in range(4):
                    @block.vector
                    def _(v, qu=qu):
                        if qu == 0:
                            ctx.wait(v, "dw", ctx.p0dw)
                            ctx.wait(v, "qidxld", qidx_ld)
                            ctx.wait(v, "cc", kv_ready)
                        if qu >= 1:
                            ctx.wait(v, "mm", ctxl_mm[qu * 32 - 1])
                        last = None
                        for ktl in range(16):
                            kt = qu * 16 + ktl
                            for qc in range(2):
                                last = v.tensor_single_scalar(
                                    masks[:, ktl * 2 + qc, :],
                                    qidx_sb[:, qc * 512:(qc + 1) * 512],
                                    kidx_sb[:, kt:kt + 1], ALU.is_ge)
                        ctx.inc(last, "vv")
                    assert ctx.cnt["vv"] == mask_vv[qu]

                    for gi in range(4):
                        sgi = qu * 4 + gi

                        @block.sync
                        def _(sy, qu=qu, gi=gi, sgi=sgi):
                            ctx.wait(sy, "cc", kv_ready)
                            if sgi >= 2:
                                ctx.wait(sy, "mm", ctxl_mm[(sgi - 1) * 8 - 1])
                            for rr in range(2):
                                r = qu * 2 + rr
                                ctx.dinc(sy.dma_start(
                                    out=kst[:, sgi % 2,
                                            rr * 1024:(rr + 1) * 1024],
                                    in_=kag_out[r * 512 + gi * 128:
                                                r * 512 + gi * 128 + 128, :]),
                                    "ds")
                            src3 = vag_out.rearrange("(kt p) dv -> p kt dv",
                                                     p=128)
                            for q4 in range(4):
                                ctx.dinc(sy.dma_start(
                                    out=vst.ap().rearrange(
                                        "p sl (kt o) -> p sl kt o",
                                        o=128)[:, sgi % 2,
                                               q4 * 4:(q4 + 1) * 4, :],
                                    in_=src3[:, qu * 16 + q4 * 4:
                                             qu * 16 + (q4 + 1) * 4,
                                             gi * 128:(gi + 1) * 128]), "ds")
                        strip_ds = ctx.cnt["ds"]

                        for hh in range(4):
                            for qc in range(2):
                                h = gi * 4 + hh
                                j = (qu * 4 + gi) * 8 + hh * 2 + qc
                                kview = kst.ap().rearrange(
                                    "p sl (kt o) -> p sl kt o", o=128)
                                vview = vst.ap().rearrange(
                                    "p sl (kt o) -> p sl kt o", o=128)

                                @block.tensor
                                def _(t, h=h, qc=qc, j=j, sgi=sgi,
                                      strip_ds=strip_ds):
                                    ctx.wait(t, "ds", strip_ds)
                                    for kt in range(16):
                                        gidx = j * 16 + kt
                                        if gidx >= 2:
                                            pj, pk = divmod(gidx - 2, 16)
                                            ctx.wait(t, "ss", exp_ss[(pj, pk)])
                                        i = t.matmul(
                                            [ps_s0, ps_s1][kt % 2][:],
                                            kview[:, sgi % 2, kt, :],
                                            qsb[:, h, qc * 512:(qc + 1) * 512],
                                            start=True, stop=True)
                                        ctx.inc(i, "mm")
                                    if j >= 2:
                                        ctx.wait(t, "vv", evac_vv[j - 2])
                                    pc = [ps_c0, ps_c1][j % 2]
                                    pl = [ps_l0, ps_l1][j % 2]
                                    for kt in range(16):
                                        ctx.wait(t, "vv", stt_vv[(j, kt)])
                                        t.matmul(pc[:], vview[:, sgi % 2, kt, :],
                                                 pmsk[:, kt, :],
                                                 start=(kt == 0),
                                                 stop=(kt == 15))
                                    for kt in range(16):
                                        i = t.matmul(pl[0:1, :], onesc[:, 0:1],
                                                     pmsk[:, kt, :],
                                                     start=(kt == 0),
                                                     stop=(kt == 15))
                                    ctx.inc(i, "mm")
                                    assert ctx.cnt["mm"] == ctxl_mm[j]

                                @block.scalar
                                def _(s, j=j):
                                    if j == 0:
                                        ctx.wait(s, "gg", 1)
                                    for kt in range(16):
                                        gidx = j * 16 + kt
                                        ctx.wait(s, "mm", sT_mm[(j, kt)])
                                        if gidx >= 2:
                                            pj, pk = divmod(gidx - 2, 16)
                                            ctx.wait(s, "vv", stt_vv[(pj, pk)])
                                        i = s.activation(
                                            out=punm[:, kt % 2, :],
                                            in_=[ps_s0, ps_s1][kt % 2][:],
                                            func=AF.Exp, bias=cbias[:, 2:3],
                                            scale=SCALE)
                                        ctx.inc(i, "ss")
                                    assert ctx.cnt["ss"] == exp_ss[(j, 15)]

                                @block.vector
                                def _(v, qu=qu, j=j, qc=qc):
                                    ctx.wait(v, "vv", mask_vv[qu])
                                    if j >= 1:
                                        ctx.wait(v, "mm", ctxl_mm[j - 1])
                                    for kt in range(16):
                                        ctx.wait(v, "ss", exp_ss[(j, kt)])
                                        i = v.scalar_tensor_tensor(
                                            pmsk[:, kt, :], punm[:, kt % 2, :],
                                            1.0, masks[:, kt * 2 + qc, :],
                                            ALU.mult, ALU.mult)
                                        ctx.inc(i, "vv")
                                    assert ctx.cnt["vv"] == stt_vv[(j, 15)]

                                @block.vector
                                def _(v, qu=qu, h=h, qc=qc, j=j):
                                    ctx.wait(v, "mm", ctxl_mm[j])
                                    if j >= 2:
                                        ctx.wait(v, lrow_ds[j - 2][0],
                                                 lrow_ds[j - 2][1])
                                    pc = [ps_c0, ps_c1][j % 2]
                                    pl = [ps_l0, ps_l1][j % 2]
                                    qs = slice(qc * 512, (qc + 1) * 512)
                                    if qu == 0:
                                        ctx.inc(v.tensor_copy(
                                            ctx_acc[:, h, qs], pc[:]), "vv")
                                    else:
                                        ctx.inc(v.tensor_tensor(
                                            ctx_acc[:, h, qs], ctx_acc[:, h, qs],
                                            pc[:], ALU.add), "vv")
                                    ctx.inc(v.tensor_copy(
                                        lrow[0:1, j % 2, :], pl[0:1, :]), "vv")
                                    assert ctx.cnt["vv"] == evac_vv[j]

                                lsm2 = ctx.new_sem(f"lr{j % 2}")

                                @block.sync
                                def _(sy, qu=qu, h=h, qc=qc, j=j, lsm2=lsm2):
                                    ctx.wait(sy, "vv", evac_vv[j])
                                    ctx.inc(sy.dma_start(
                                        out=l_dram[qu * 32 + h * 2 + qc:
                                                   qu * 32 + h * 2 + qc + 1, :],
                                        in_=lrow[0:1, j % 2, :]), lsm2, 16)
                                lrow_ds.append((lsm2, ctx.cnt[lsm2]))
                attn_done = ctx.cnt["vv"]
                last_l0 = ctx.cnt["lr0"]
                last_l1 = ctx.cnt["lr1"]
                phQ.close()

                with contextlib.ExitStack() as ph3:
                    l_all = sbuf("p1_lall", [1, 16 * S], F32, ph3)
                    linvb = sbuf("p1_linvb", [1, 2, 512], BF16, ph3)
                    cbf = sbuf("p1_cbf", [128, 16, S], BF16, ph3)
                    ps_r0 = psum("p1_pr0", [128, 512], ph3)
                    ps_r1 = psum("p1_pr1", [128, 512], ph3)

                    @block.gpsimd
                    def _(g):
                        ctx.wait(g, "lr0", last_l0)
                        ctx.wait(g, "lr1", last_l1)
                        for qu in range(4):
                            i = g.dma_start(
                                out=l_all[0:1, :],
                                in_=l_dram[qu * 32:(qu + 1) * 32, :]
                                .rearrange("r q -> (r q)").unsqueeze(0),
                                accum_op=(ALU.bypass if qu == 0 else ALU.add))
                            ctx.dinc(i, "dw")
                            ctx.wait(g, "dw")
                    lsum_dw = ctx.cnt["dw"]

                    @block.vector
                    def _(v):
                        ctx.wait(v, "dw", lsum_dw)
                        ctx.inc(v.reciprocal(l_all[0:1, :], l_all[0:1, :]),
                                "vv")
                    linv_done = ctx.cnt["vv"]

                    scale_vv = []
                    cast_vv = []
                    for h in range(16):
                        for qc in range(2):
                            idx = len(scale_vv)

                            @block.vector
                            def _(v, h=h, qc=qc, idx=idx):
                                ctx.wait(v, "vv", linv_done)
                                off = h * S + qc * 512
                                ctx.inc(v.tensor_copy(
                                    linvb[0:1, idx % 2, :],
                                    l_all[0:1, off:off + 512]), "vv")
                            cast_vv.append(ctx.cnt["vv"])

                            @block.tensor
                            def _(t, h=h, qc=qc, idx=idx):
                                ctx.wait(t, "vv", cast_vv[idx])
                                if idx >= 2:
                                    ctx.wait(t, "vv", scale_vv[idx - 2])
                                pp = [ps_r0, ps_r1][idx % 2]
                                ctx.inc(t.matmul(
                                    pp[:], ones1[0:1, :],
                                    linvb[0:1, idx % 2, :],
                                    start=True, stop=True), "mm")
                            mmv = ctx.cnt["mm"]

                            @block.vector
                            def _(v, h=h, qc=qc, mmv=mmv, idx=idx):
                                ctx.wait(v, "mm", mmv)
                                pp = [ps_r0, ps_r1][idx % 2]
                                qs = slice(qc * 512, (qc + 1) * 512)
                                ctx.inc(v.tensor_tensor(
                                    cbf[:, h, qs], ctx_acc[:, h, qs], pp[:],
                                    ALU.mult), "vv")
                            scale_vv.append(ctx.cnt["vv"])
                    scaled = ctx.cnt["vv"]

                    ws_d = WStream("p1_wsd", ph3, wdense_p, 16)
                    est = new_evac_state()
                    for ot in range(16):
                        @block.gpsimd
                        def _(g, ot=ot):
                            rel = (est["add"][2 * (ot - 2) + 1]
                                   if ot >= 2 else scaled)
                            ws_d.load(g, ot, rel)
                        ld = ws_d.loaded[ot]

                        for qc in range(2):
                            idx = 2 * ot + qc

                            @block.tensor
                            def _(t, ot=ot, qc=qc, ld=ld, idx=idx):
                                ctx.wait(t, ld[0], ld[1])
                                ctx.wait(t, "vv", scaled)
                                if idx >= 2:
                                    ctx.wait(t, "vv", est["add"][idx - 2])
                                pp = [ps_r0, ps_r1][idx % 2]
                                for ct in range(16):
                                    i = t.matmul(
                                        pp[:], ws_d.tile(ot, ct),
                                        cbf[:, ct, qc * 512:(qc + 1) * 512],
                                        start=(ct == 0), stop=(ct == 15))
                                ctx.inc(i, "mm")
                            mmv = ctx.cnt["mm"]
                            res_evac(est, ot, qc, [ps_r0, ps_r1][idx % 2],
                                     mmv, [x_slab], xres)

        def phase_ffn(tagn, src_dram, dst_dram):
            with contextlib.ExitStack() as ph:
                G = sbuf(f"f{tagn}_G", [128, 64, S], BF16, ph)
                phL = ph.enter_context(contextlib.ExitStack())
                lnx = sbuf(f"f{tagn}_lnx", [128, 16, S], BF16, phL)
                ln_phase(src_dram, lnx, tag=f"f{tagn}")
                lnx_done = ctx.cnt["vv"]
                ps = [psum(f"f{tagn}_ps{i}", [128, 512], ph) for i in range(6)]

                sg = sbuf(f"f{tagn}_sg", [128, 2, 512], BF16, phL)
                ws_1 = WStream(f"f{tagn}_w1", phL, w1_p, 16, slots=2)
                ws_3 = WStream(f"f{tagn}_w3", phL, w3_p, 16, slots=2)

                ev_h = []
                for ft in range(64):
                    @block.gpsimd
                    def _(g, ft=ft):
                        if ft == 0:
                            ctx.wait(g, "vv")
                            ctx.wait(g, "ss")
                        rel = ev_h[2 * (ft - 2) + 1] if ft >= 2 else None
                        ws_1.load(g, ft, rel)
                        ws_3.load(g, ft)
                    ld = ws_3.loaded[ft]

                    for qc in range(2):
                        idx = 2 * ft + qc

                        @block.tensor
                        def _(t, ft=ft, qc=qc, ld=ld, idx=idx):
                            ctx.wait(t, ld[0], ld[1])
                            ctx.wait(t, "vv", lnx_done)
                            if idx >= 2:
                                ctx.wait(t, "vv", ev_h[idx - 2])
                            pg = ps[2 * (idx % 2)]
                            pl = ps[2 * (idx % 2) + 1]
                            for ct in range(16):
                                t.matmul(pg[:], ws_1.tile(ft, ct),
                                         lnx[:, ct, qc * 512:(qc + 1) * 512],
                                         start=(ct == 0), stop=(ct == 15))
                            for ct in range(16):
                                i = t.matmul(pl[:], ws_3.tile(ft, ct),
                                             lnx[:, ct, qc * 512:(qc + 1) * 512],
                                             start=(ct == 0), stop=(ct == 15))
                            ctx.inc(i, "mm")
                        mmv = ctx.cnt["mm"]

                        @block.scalar
                        def _(s, mmv=mmv, idx=idx):
                            ctx.wait(s, "mm", mmv)
                            if idx >= 2:
                                ctx.wait(s, "vv", ev_h[idx - 2])
                            ctx.inc(s.activation(out=sg[:, idx % 2, :],
                                                 in_=ps[2 * (idx % 2)][:],
                                                 func=AF.Silu), "ss")
                        ssv = ctx.cnt["ss"]

                        @block.vector
                        def _(v, ft=ft, qc=qc, ssv=ssv, idx=idx):
                            ctx.wait(v, "ss", ssv)
                            ctx.inc(v.tensor_tensor(
                                G[:, ft, qc * 512:(qc + 1) * 512],
                                sg[:, idx % 2, :], ps[2 * (idx % 2) + 1][:],
                                ALU.mult), "vv")
                        ev_h.append(ctx.cnt["vv"])
                hidden_done = ctx.cnt["vv"]

                st1 = []
                for a in range(64):
                    for qc in range(2):
                        idx = 2 * a + qc

                        @block.tensor
                        def _(t, a=a, qc=qc, idx=idx):
                            ctx.wait(t, "vv",
                                     hidden_done if idx < 2 else st1[idx - 2])
                            pp = ps[4 + idx % 2]
                            ctx.inc(t.matmul(
                                pp[:], h128_sb[:],
                                G[:, a, qc * 512:(qc + 1) * 512],
                                start=True, stop=True), "mm")
                        mmv = ctx.cnt["mm"]

                        @block.vector
                        def _(v, a=a, qc=qc, mmv=mmv, idx=idx):
                            ctx.wait(v, "mm", mmv)
                            ctx.inc(v.tensor_copy(
                                G[:, a, qc * 512:(qc + 1) * 512],
                                ps[4 + idx % 2][:]), "vv")
                        st1.append(ctx.cnt["vv"])
                g1_done = ctx.cnt["vv"]
                phL.close()

                w2sb = sbuf(f"f{tagn}_w2s", [128, 2, 64 * 128], BF16, ph)
                est = new_evac_state()
                w2_mm = []
                for ot in range(16):
                    wsm = ctx.new_sem(f"w2s{ot % 2}")

                    @block.sync
                    def _(sy, ot=ot, wsm=wsm):
                        ctx.wait(sy, "cc", w2h_ready)
                        if ot >= 2:
                            ctx.wait(sy, "mm", w2_mm[2 * (ot - 2) + 1])
                        else:
                            ctx.wait(sy, "vv", g1_done)
                        r = ot // 2
                        row0 = r * 256 + (ot % 2) * 128
                        ctx.inc(sy.dma_start(
                            out=w2sb[:, ot % 2, :],
                            in_=w2h_out[row0:row0 + 128, :]), wsm, 16)
                    ld = (wsm, ctx.cnt[wsm])

                    for qc in range(2):
                        idx = 2 * ot + qc

                        @block.tensor
                        def _(t, ot=ot, qc=qc, ld=ld, idx=idx):
                            ctx.wait(t, ld[0], ld[1])
                            ctx.wait(t, "vv", g1_done)
                            if idx >= 2:
                                ctx.wait(t, "vv", est["add"][idx - 2])
                            pp = ps[4 + idx % 2]
                            wv = w2sb.ap().rearrange(
                                "p sl (a o) -> p sl a o", o=128)
                            for ct in range(64):
                                i = t.matmul(pp[:], wv[:, ot % 2, ct, :],
                                             G[:, ct, qc * 512:(qc + 1) * 512],
                                             start=(ct == 0), stop=(ct == 63))
                            ctx.inc(i, "mm")
                        w2_mm.append(ctx.cnt["mm"])
                        res_evac(est, ot, qc, ps[4 + idx % 2], w2_mm[-1],
                                 [src_dram], dst_dram)

        def phase_mamba():
            with contextlib.ExitStack() as ph:
                XC = sbuf("m_xc", [128, 32, S], BF16, ph)

                ph2 = ph.enter_context(contextlib.ExitStack())
                if True:
                    lnx1 = sbuf("m_lnx1", [128, 16, S], BF16, ph2)
                    with contextlib.ExitStack() as ph3:
                        lnx0 = sbuf("m_lnx0", [128, 16, S], BF16, ph3)
                        ln_phase(xres, lnx0, tag="m0", f32_dst=lx_dram)
                        lnx0_done = ctx.cnt["vv"]

                        xqs = sbuf("m1_xq", [128, 2, 512], BF16, ph3)
                        x2s = sbuf("m1_x2", [128, 2, 512], F32, ph3)
                        row = sbuf("m1_row", [1, 8 * 512], F32, ph3)
                        rbf = sbuf("m1_rbf", [1, 2 * 512], BF16, ph3)
                        ps_m = psum("m1_pm", [1, 512], ph3)
                        ps_q = psum("m1_pq", [1, 512], ph3)
                        ps_mr = psum("m1_pmr", [128, 512], ph3)
                        ps_rr = psum("m1_prr", [128, 512], ph3)
                        for qc in range(2):
                            qs = slice(qc * 512, qc * 512 + 512)
                            sq_v = []
                            for ct in range(16):
                                sl = ct % 2

                                @block.scalar
                                def _(s, ct=ct, sl=sl, qc=qc):
                                    ctx.wait(s, "vv", lnx0_done)
                                    if ct >= 2 or qc > 0:
                                        ctx.wait(s, "mm")
                                    ctx.inc(s.activation(out=xqs[:, sl, :],
                                                         in_=lnx0[:, ct, qs],
                                                         func=AF.Square), "ss")
                                sq_v.append(ctx.cnt["ss"])

                                @block.tensor
                                def _(t, ct=ct, sl=sl, qc=qc, q=sq_v[-1]):
                                    ctx.wait(t, "vv", lnx0_done)
                                    if ct == 0 and qc > 0:
                                        ctx.wait(t, "vv")
                                    t.matmul(ps_m[0:1, :], onesc[:],
                                             lnx0[:, ct, qs],
                                             start=(ct == 0), stop=(ct == 15))
                                    ctx.wait(t, "ss", q)
                                    i = t.matmul(ps_q[0:1, :], onesc[:],
                                                 xqs[:, sl, :],
                                                 start=(ct == 0),
                                                 stop=(ct == 15))
                                    ctx.inc(i, "mm")
                            stats = ctx.cnt["mm"]

                            @block.vector
                            def _(v, stats=stats):
                                ctx.wait(v, "mm", stats)
                                v.tensor_single_scalar(row[0:1, 0:512],
                                                       ps_m[0:1, :],
                                                       1.0 / D, ALU.mult)
                                v.tensor_single_scalar(row[0:1, 512:1024],
                                                       ps_q[0:1, :],
                                                       1.0 / D, ALU.mult)
                                v.tensor_mul(row[0:1, 1024:1536],
                                             row[0:1, 0:512], row[0:1, 0:512])
                                ctx.inc(v.tensor_sub(row[0:1, 1536:2048],
                                                     row[0:1, 512:1024],
                                                     row[0:1, 1024:1536]),
                                        "vv")
                            var_done = ctx.cnt["vv"]

                            @block.scalar
                            def _(s, var_done=var_done):
                                ctx.wait(s, "vv", var_done)
                                s.activation(out=row[0:1, 2048:2560],
                                             in_=row[0:1, 1536:2048],
                                             func=AF.Ln, bias=cbias[0:1, 0:1])
                                ctx.inc(s.activation(out=row[0:1, 2560:3072],
                                                     in_=row[0:1, 2048:2560],
                                                     func=AF.Exp, scale=-0.5),
                                        "ss")
                            rstd = ctx.cnt["ss"]

                            @block.vector
                            def _(v, rstd=rstd):
                                ctx.wait(v, "ss", rstd)
                                ctx.wait(v, "mm")
                                v.tensor_copy(rbf[0:1, 0:512], row[0:1, 0:512])
                                ctx.inc(v.tensor_copy(rbf[0:1, 512:1024],
                                                      row[0:1, 2560:3072]),
                                        "vv")
                            rows_bf = ctx.cnt["vv"]

                            @block.tensor
                            def _(t, rows_bf=rows_bf):
                                ctx.wait(t, "vv", rows_bf)
                                t.matmul(ps_mr[:], ones1[0:1, :],
                                         rbf[0:1, 0:512], start=True, stop=True)
                                ctx.inc(t.matmul(ps_rr[:], ones1[0:1, :],
                                                 rbf[0:1, 512:1024],
                                                 start=True, stop=True), "mm")
                            reps = ctx.cnt["mm"]

                            for ct in range(16):
                                sl = ct % 2

                                @block.vector
                                def _(v, ct=ct, sl=sl, reps=reps):
                                    ctx.wait(v, "mm", reps)
                                    v.tensor_sub(x2s[:, sl, :], lnx0[:, ct, qs],
                                                 ps_mr[:])
                                    v.tensor_tensor(x2s[:, sl, :], x2s[:, sl, :],
                                                    ps_rr[:], ALU.mult)
                                    ctx.inc(v.tensor_scalar(
                                        lnx1[:, ct, qs], x2s[:, sl, :],
                                        ln1w_sb[:, ct:ct + 1],
                                        ln1b_sb[:, ct:ct + 1],
                                        ALU.mult, ALU.add), "vv")
                    lnx1_done = ctx.cnt["vv"]
                    XM = sbuf("m_xm", [128, 32, 1027], BF16, ph2)

                    ps = [psum(f"m_ps{i}", [128, 512], ph2) for i in range(4)]
                    zst = sbuf("m_zst", [128, 2, 512], BF16, ph2)
                    ws_i = WStream("m_wsi", ph2, winp_p, 16, slots=2)
                    ev_i = []
                    zstore = []
                    for ot in range(64):
                        @block.gpsimd
                        def _(g, ot=ot):
                            if ot >= 2:
                                sem, cnt = ev_i[2 * (ot - 2) + 1]
                                g.wait_ge(ctx.sem[sem], cnt)
                            else:
                                ctx.wait(g, "vv", lnx1_done)
                            ws_i.load(g, ot)
                        ld = ws_i.loaded[ot]

                        for qc in range(2):
                            idx = 2 * ot + qc

                            @block.tensor
                            def _(t, ot=ot, qc=qc, ld=ld, idx=idx):
                                ctx.wait(t, ld[0], ld[1])
                                ctx.wait(t, "vv", lnx1_done)
                                if idx >= 2:
                                    sem, cnt = ev_i[idx - 2]
                                    ctx.wait(t, sem, cnt)
                                pp = ps[idx % 2]
                                for ct in range(16):
                                    i = t.matmul(
                                        pp[:], ws_i.tile(ot, ct),
                                        lnx1[:, ct, qc * 512:(qc + 1) * 512],
                                        start=(ct == 0), stop=(ct == 15))
                                ctx.inc(i, "mm")
                            mmv = ctx.cnt["mm"]

                            if ot < 32:
                                @block.vector
                                def _(v, ot=ot, qc=qc, mmv=mmv, idx=idx):
                                    ctx.wait(v, "mm", mmv)
                                    ctx.inc(v.tensor_copy(
                                        XM[:, ot,
                                           3 + qc * 512:3 + qc * 512 + 512],
                                        ps[idx % 2][:]), "vv")
                                ev_i.append(("vv", ctx.cnt["vv"]))
                            else:
                                zsm = ctx.new_sem(f"zs{idx % 2}")

                                @block.scalar
                                def _(s, mmv=mmv, idx=idx):
                                    ctx.wait(s, "mm", mmv)
                                    if len(zstore) >= 2:
                                        ctx.wait(s, zstore[-2][0],
                                                 zstore[-2][1])
                                    ctx.inc(s.activation(
                                        out=zst[:, idx % 2, :],
                                        in_=ps[idx % 2][:], func=AF.Silu),
                                        "ss")
                                ev_i.append(("ss", ctx.cnt["ss"]))

                                @block.sync
                                def _(sy, ot=ot, qc=qc, idx=idx, zsm=zsm):
                                    ctx.wait(sy, "ss", ev_i[-1][1])
                                    ctx.inc(sy.dma_start(
                                        out=sz_dram[
                                            (ot - 32) * 128:(ot - 31) * 128,
                                            qc * 512:(qc + 1) * 512],
                                        in_=zst[:, idx % 2, :]), zsm, 16)
                                zstore.append((zsm, ctx.cnt[zsm]))
                    inproj_vv = ctx.cnt["vv"]

                    @block.sync
                    def _(sy):
                        ctx.wait(sy, "vv", inproj_vv)
                        ctx.dinc(sy.dma_start(
                            out=edge_in.rearrange("a (p dt t) -> (a p) dt t",
                                                  t=3, p=128),
                            in_=XM.ap()[:, :, 1024:1027]), "ds")
                    edge_pub = ctx.cnt["ds"]

                    @block.gpsimd
                    def _(g):
                        ctx.wait(g, "ds", edge_pub)
                        ctx.cinc(g.collective_compute(
                            "AllGather", ALU.bypass,
                            replica_groups=[list(range(NCORES))],
                            ins=[edge_in.opt()], outs=[edge_out.opt()]), "cc")
                    edge_ag = ctx.cnt["cc"]

                    edsb = sbuf("m_edsb", [8, 128 * 24], BF16, ph2)
                    ps_e = psum("m_pse", [128, 128], ph2)
                    edv_dram = edge_out.rearrange("r (p c) -> r p c", p=128)
                    ed_mm_l = []
                    for ec in range(4):
                        @block.sync
                        def _(sy, ec=ec):
                            ctx.wait(sy, "cc", edge_ag)
                            if ec >= 1:
                                ctx.wait(sy, "mm", ed_mm_l[-1])
                            ctx.dinc(sy.dma_start(
                                out=edsb.ap().rearrange(
                                    "r (p c) -> r p c", c=24),
                                in_=edv_dram[:, :, ec * 24:(ec + 1) * 24]),
                                "ds")
                        ed_ld = ctx.cnt["ds"]

                        @block.tensor
                        def _(t, ec=ec, ed_ld=ed_ld):
                            ctx.wait(t, "ds", ed_ld)
                            ctx.wait(t, "dw", ctx.p0dw)
                            last = None
                            edv = edsb.ap().rearrange("r (p c) -> r p c",
                                                      c=24)
                            for ci in range(24):
                                col = ec * 24 + ci
                                last = t.matmul(
                                    ps_e[:, col:col + 1],
                                    edv[0:8, :, ci],
                                    sele_sb[0:8, 0:1],
                                    start=True, stop=True)
                            ctx.inc(last, "mm")
                        ed_mm_l.append(ctx.cnt["mm"])
                    ed_mm = ed_mm_l[-1]

                    @block.vector
                    def _(v, ed_mm=ed_mm):
                        ctx.wait(v, "mm", ed_mm)
                        ctx.inc(v.tensor_copy(
                            XM.ap()[:, :, 0:3],
                            ps_e.ap()[:, 0:96].rearrange("p (dt t) -> p dt t",
                                                         t=3)), "vv")
                    halo_done = ctx.cnt["vv"]

                    acc = sbuf("m_acc", [128, 2, S], F32, ph2)
                    dxst = sbuf("m_dxst", [128, 2, S], BF16, ph2)
                    xc_ss, dx_store = [], []
                    for dt in range(32):
                        sl = dt % 2

                        @block.vector
                        def _(v, dt=dt, sl=sl):
                            ctx.wait(v, "vv", halo_done)
                            if dt == 0:
                                ctx.wait(v, "dw", ctx.p0dw)
                            if dt >= 2:
                                ctx.wait(v, "ss", xc_ss[dt - 2])
                            v.tensor_scalar(acc[:, sl, :], XM[:, dt, 0:1024],
                                            convw_sb[:, dt * 4:dt * 4 + 1],
                                            None, ALU.mult)
                            i = None
                            for tap in range(1, 4):
                                i = v.scalar_tensor_tensor(
                                    acc[:, sl, :], XM[:, dt, tap:tap + 1024],
                                    convw_sb[:, dt * 4 + tap:dt * 4 + tap + 1],
                                    acc[:, sl, :], ALU.mult, ALU.add)
                            ctx.inc(i, "vv")
                        cvv = ctx.cnt["vv"]

                        @block.scalar
                        def _(s, dt=dt, sl=sl, cvv=cvv):
                            ctx.wait(s, "vv", cvv)
                            if dt == 0:
                                ctx.wait(s, "dw", ctx.p0dw)
                            ctx.inc(s.activation(out=XC[:, dt, :],
                                                 in_=acc[:, sl, :],
                                                 func=AF.Silu,
                                                 bias=convb_sb[:, dt:dt + 1]),
                                    "ss")
                        xc_ss.append(ctx.cnt["ss"])

                        dsm = ctx.new_sem(f"dx{sl}")

                        @block.vector
                        def _(v, dt=dt, sl=sl):
                            ctx.wait(v, "ss", xc_ss[-1])
                            if dt >= 2:
                                ctx.wait(v, dx_store[dt - 2][0],
                                         dx_store[dt - 2][1])
                            ctx.inc(v.tensor_scalar(
                                dxst[:, sl, :], XC[:, dt, :],
                                dsk_sb[:, dt:dt + 1], None, ALU.mult), "vv")
                        dx_vv = ctx.cnt["vv"]

                        @block.sync
                        def _(sy, dt=dt, sl=sl, dx_vv=dx_vv, dsm=dsm):
                            ctx.wait(sy, "vv", dx_vv)
                            ctx.inc(sy.dma_start(
                                out=dxc_dram[dt * 128:(dt + 1) * 128, :],
                                in_=dxst[:, sl, :]), dsm, 16)
                        dx_store.append((dsm, ctx.cnt[dsm]))
                    conv_done = ctx.cnt["ss"]
                    ph2.close()

                    ph2b = ph.enter_context(contextlib.ExitStack())
                    DT = sbuf("m_dt", [128, 32, S], BF16, ph)
                    bc = sbuf("m_bc", [32, S], BF16, ph)
                    ps2 = [psum(f"m_ps2{i}", [128, 512], ph2b)
                           for i in range(4)]

                    dtr = sbuf("m_dtr", [128, S], BF16, ph2b)
                    ws_x = WStream("m_wsx", ph2b, wxp_p, 32, slots=2)
                    for ot in range(2):
                        @block.gpsimd
                        def _(g, ot=ot):
                            if ot == 0:
                                ctx.wait(g, "vv")
                                ctx.wait(g, dx_store[-1][0], dx_store[-1][1])
                                ctx.wait(g, dx_store[-2][0], dx_store[-2][1])
                            ws_x.load(g, ot)
                        ld = ws_x.loaded[ot]

                        for qc in range(2):
                            @block.tensor
                            def _(t, ot=ot, qc=qc, ld=ld):
                                ctx.wait(t, ld[0], ld[1])
                                ctx.wait(t, "ss", conv_done)
                                pp = ps2[2 + qc]
                                if ot == 1:
                                    ctx.wait(t, "vv")
                                for ct in range(32):
                                    i = t.matmul(
                                        pp[:], ws_x.tile(ot, ct),
                                        XC[:, ct, qc * 512:(qc + 1) * 512],
                                        start=(ct == 0), stop=(ct == 31))
                                ctx.inc(i, "mm")
                            mmv = ctx.cnt["mm"]

                            @block.vector
                            def _(v, ot=ot, qc=qc, mmv=mmv):
                                ctx.wait(v, "mm", mmv)
                                qs = slice(qc * 512, (qc + 1) * 512)
                                if ot == 0:
                                    ctx.inc(v.tensor_copy(dtr[:, qs],
                                                          ps2[2 + qc][:]), "vv")
                                else:
                                    ctx.inc(v.tensor_copy(
                                        bc[0:32, qs], ps2[2 + qc][0:32, :]),
                                        "vv")
                    xproj_done = ctx.cnt["vv"]
                    bcs = ctx.new_sem("bcst")

                    @block.sync
                    def _(sy):
                        ctx.wait(sy, "vv", xproj_done)
                        ctx.inc(sy.dma_start(out=bc_dram[:, :],
                                             in_=bc[0:32, :]), bcs, 16)
                    bc_stored = ctx.cnt[bcs]

                    brep_done = xproj_done

                    ws_t = WStream("m_wst", ph2b, wdtp_p, 1, slots=4)
                    spst = sbuf("m_spst", [128, 4, 512], F32, ph2b)
                    dt_vv = []
                    for ot in range(32):
                        @block.gpsimd
                        def _(g, ot=ot):
                            rel = dt_vv[2 * (ot - 3) + 1] if ot >= 3 else brep_done
                            ws_t.load(g, ot, rel)
                        ld = ws_t.loaded[ot]

                        for qc in range(2):
                            idx = 2 * ot + qc

                            @block.tensor
                            def _(t, ot=ot, qc=qc, ld=ld, idx=idx):
                                ctx.wait(t, ld[0], ld[1])
                                ctx.wait(t, "vv", brep_done)
                                if idx >= 2:
                                    ctx.wait(t, "vv", dt_vv[idx - 2])
                                pp = ps2[idx % 2]
                                ctx.inc(t.matmul(
                                    pp[:], ws_t.tile(ot, 0),
                                    dtr[:, qc * 512:(qc + 1) * 512],
                                    start=True, stop=True), "mm")
                            mmv = ctx.cnt["mm"]

                            @block.vector
                            def _(v, ot=ot, mmv=mmv, idx=idx):
                                ctx.wait(v, "mm", mmv)
                                if idx == 0:
                                    ctx.wait(v, "dw", ctx.p0dw)
                                b = 2 * (idx % 2)
                                ctx.inc(v.tensor_scalar(
                                    spst[:, b, :], ps2[idx % 2][:],
                                    dtb_sb[:, ot:ot + 1], None, ALU.add), "vv")
                            pre_vv = ctx.cnt["vv"]

                            @block.scalar
                            def _(s, pre_vv=pre_vv, idx=idx):
                                ctx.wait(s, "vv", pre_vv)
                                if idx == 0:
                                    ctx.wait(s, "gg", 1)
                                b = 2 * (idx % 2)
                                s.activation(out=spst[:, b + 1, :],
                                             in_=spst[:, b, :], func=AF.Abs)
                                s.activation(out=spst[:, b + 1, :],
                                             in_=spst[:, b + 1, :],
                                             func=AF.Exp, scale=-1.0)
                                ctx.inc(s.activation(out=spst[:, b + 1, :],
                                                     in_=spst[:, b + 1, :],
                                                     func=AF.Ln, bias=cbias[:, 1:2]),
                                        "ss")
                            sp_ss = ctx.cnt["ss"]

                            @block.vector
                            def _(v, ot=ot, qc=qc, sp_ss=sp_ss, idx=idx):
                                ctx.wait(v, "ss", sp_ss)
                                b = 2 * (idx % 2)
                                v.tensor_relu(spst[:, b, :], spst[:, b, :])
                                ctx.inc(v.tensor_tensor(
                                    DT[:, ot, qc * 512:(qc + 1) * 512],
                                    spst[:, b, :], spst[:, b + 1, :],
                                    ALU.add), "vv")
                            dt_vv.append(ctx.cnt["vv"])
                    dt_done = ctx.cnt["vv"]

                    @block.vector
                    def _(v):
                        ctx.wait(v, "vv", dt_done)
                        ctx.wait(v, dx_store[-1][0], dx_store[-1][1])
                        ctx.wait(v, dx_store[-2][0], dx_store[-2][1])
                        ctx.inc(v.tensor_tensor(
                            XC.ap().rearrange("p dt s -> p (dt s)"),
                            DT.ap().rearrange("p dt s -> p (dt s)"),
                            XC.ap().rearrange("p dt s -> p (dt s)"),
                            ALU.mult), "vv")
                    u_done = ctx.cnt["vv"]
                    ph2b.close()

                ones_t = sbuf("m_onest", [128, S], BF16, ph)
                BR = sbuf("m_br", [128, 8, S], BF16, ph)
                CR = sbuf("m_cr", [128, 8, S], BF16, ph)
                hin = sbuf("m_hin", [128, 32, NST], F32, ph)
                @block.vector
                def _(v):
                    v.memset(ones_t[:], 1.0)

                brs = ctx.new_sem("brep")

                def build_reps(sh, want_c):
                    pre_vv = ctx.cnt["vv"]

                    @block.sync
                    def _(sy, sh=sh, want_c=want_c, pre_vv=pre_vv):
                        ctx.wait(sy, "bcst", bc_stored)
                        ctx.wait(sy, "vv", pre_vv)
                        for si in range(8):
                            ctx.inc(sy.dma_start(
                                out=BR[:, si, :],
                                in_=bc_dram[sh * 8 + si:sh * 8 + si + 1, :]
                                .partition_broadcast(128).squeeze(1)),
                                brs, 16)
                            if want_c:
                                ctx.inc(sy.dma_start(
                                    out=CR[:, si, :],
                                    in_=bc_dram[16 + sh * 8 + si:
                                                17 + sh * 8 + si, :]
                                    .partition_broadcast(128).squeeze(1)),
                                    brs, 16)
                    return ("brep", ctx.cnt[brs])

                with contextlib.ExitStack() as phs1:
                    ssum = sbuf("m_ssum", [128, 32, NST], F32, phs1)
                    ptot = sbuf("m_ptot", [128, 32 * NST], F32, phs1)
                    dtot = sbuf("m_dtot", [128, 32], F32, phs1)
                    est_s = sbuf("m_est", [128, 2, S], BF16, phs1)
                    bst = sbuf("m_bst", [128, 2, S], BF16, phs1)
                    dcst = sbuf("m_dcst", [128, 2, S], F32, phs1)
                    trash = sbuf("m_trash", [128, S], F32, phs1)

                    for sh in range(2):
                        reps1 = build_reps(sh, False)
                        s1_ss, s1_vv = {}, {}
                        sm, sv = ctx.cnt["ss"], ctx.cnt["vv"]
                        for dt in range(32):
                            sv += 3
                            s1_vv[(dt, "r")] = sv
                            for si in range(8):
                                sm += 1
                                s1_ss[(dt, si)] = sm
                                sv += 2
                                s1_vv[(dt, si)] = sv
                        for dt in range(32):
                            sl = dt % 2

                            @block.vector
                            def _(v, dt=dt, sl=sl, sh=sh, reps1=reps1,
                                  s1_ss=s1_ss):
                                ctx.wait(v, reps1[0], reps1[1])
                                if dt == 0 and sh == 0:
                                    ctx.wait(v, "gg", 1)
                                if dt >= 2:
                                    ctx.wait(v, "ss", s1_ss[(dt - 2, 7)])
                                v.tensor_tensor_scan(
                                    dcst[:, sl, :], ones_t[:, :],
                                    DT[:, dt, :], 0.0, ALU.mult, ALU.add)
                                v.tensor_copy(dtot[:, dt:dt + 1],
                                              dcst[:, sl, 1023:1024])
                                ctx.inc(v.tensor_scalar(
                                    dcst[:, sl, :], dcst[:, sl, :],
                                    dcst[:, sl, 1023:1024], -1.0,
                                    ALU.subtract, ALU.mult), "vv", 3)
                            assert ctx.cnt["vv"] == s1_vv[(dt, "r")]

                            for si in range(8):
                                gsx = dt * 8 + si
                                s_i = sh * 8 + si

                                @block.scalar
                                def _(s, dt=dt, si=si, sl=sl, gsx=gsx,
                                      s_i=s_i, s1_vv=s1_vv):
                                    ctx.wait(s, "vv", s1_vv[(dt, "r")])
                                    if gsx >= 2:
                                        pd, psi = divmod(gsx - 2, 8)
                                        ctx.wait(s, "vv", s1_vv[(pd, psi)])
                                    elif sh == 1:
                                        ctx.wait(s, "vv")
                                    ctx.inc(s.activation(
                                        out=est_s[:, gsx % 2, :],
                                        in_=dcst[:, sl, :], func=AF.Exp,
                                        scale=a_sb[:, dt * 16 + s_i:
                                                   dt * 16 + s_i + 1]), "ss")
                                assert ctx.cnt["ss"] == s1_ss[(dt, si)]

                                @block.vector
                                def _(v, dt=dt, si=si, gsx=gsx, s_i=s_i,
                                      s1_ss=s1_ss):
                                    ctx.wait(v, "ss", s1_ss[(dt, si)])
                                    v.tensor_tensor(bst[:, gsx % 2, :],
                                                    XC[:, dt, :],
                                                    BR[:, si, :], ALU.mult)
                                    ctx.inc(v.scalar_tensor_tensor(
                                        trash[:, :], est_s[:, gsx % 2, :],
                                        1.0, bst[:, gsx % 2, :],
                                        ALU.mult, ALU.mult,
                                        accum_out=ssum[:, dt, s_i:s_i + 1]),
                                        "vv", 2)
                                assert ctx.cnt["vv"] == s1_vv[(dt, si)]
                    s1_done = ctx.cnt["vv"]

                    @block.vector
                    def _(v):
                        ctx.wait(v, "vv", s1_done)
                        last = None
                        for dt in range(32):
                            last = v.tensor_scalar(
                                ptot[:, dt * 16:(dt + 1) * 16],
                                a_sb[:, dt * 16:(dt + 1) * 16],
                                dtot[:, dt:dt + 1], None, ALU.mult)
                        ctx.inc(last, "vv")
                    pt_mul = ctx.cnt["vv"]

                    @block.scalar
                    def _(s):
                        ctx.wait(s, "vv", pt_mul)
                        ctx.inc(s.activation(out=ptot[:], in_=ptot[:],
                                             func=AF.Exp), "ss")
                    pt_done = ctx.cnt["ss"]

                    @block.sync
                    def _(sy):
                        ctx.wait(sy, "vv", s1_done)
                        ctx.wait(sy, "ss", pt_done)
                        ctx.dinc(sy.dma_start(
                            out=sum_in[:, 0:512].rearrange(
                                "p (dt s) -> p dt s", s=NST),
                            in_=ssum[:, :, :]), "ds")
                        ctx.dinc(sy.dma_start(
                            out=sum_in[:, 512:1024],
                            in_=ptot[:, :]), "ds")
                    sum_pub = ctx.cnt["ds"]

                @block.gpsimd
                def _(g):
                    ctx.wait(g, "ds", sum_pub)
                    ctx.cinc(g.collective_compute(
                        "AllGather", ALU.bypass,
                        replica_groups=[list(range(NCORES))],
                        ins=[sum_in.opt()], outs=[sum_out.opt()]), "cc")
                sum_ag = ctx.cnt["cc"]

                phC = ph.enter_context(contextlib.ExitStack())
                hrun = sbuf("m_hrun", [128, 32, NST], F32, phC)
                spsl = sbuf("m_spsl", [128, 2, 32 * 2 * NST], F32, phC)
                sp_lds = []
                comb_vv = []
                for b in range(8):
                    ssm = ctx.new_sem(f"sp{b % 2}")

                    @block.sync
                    def _(sy, b=b, ssm=ssm):
                        ctx.wait(sy, "cc", sum_ag)
                        if b >= 2:
                            ctx.wait(sy, "vv", comb_vv[b - 2])
                        ctx.inc(sy.dma_start(
                            out=spsl[:, b % 2, :],
                            in_=sum_out[b * 128:(b + 1) * 128, :]), ssm, 16)
                    sp_lds.append((ssm, ctx.cnt[ssm]))

                    @block.vector
                    def _(v, b=b):
                        ctx.wait(v, sp_lds[b][0], sp_lds[b][1])
                        if b == 0:
                            ctx.wait(v, "dw", ctx.p0dw)
                        sview = spsl.ap().rearrange(
                            "p sl (half dt s) -> p sl half dt s",
                            half=2, s=NST)
                        if b == 0:
                            v.memset(hrun[:, :, :], 0.0)
                            v.memset(hin[:, :, :], 0.0)
                        v.tensor_tensor(hrun[:, :, :], hrun[:, :, :],
                                        sview[:, b % 2, 1, :, :], ALU.mult)
                        v.tensor_tensor(hrun[:, :, :], hrun[:, :, :],
                                        sview[:, b % 2, 0, :, :], ALU.add)
                        i = v.scalar_tensor_tensor(
                            hin.ap().rearrange("p dt s -> p (dt s)"),
                            hrun.ap().rearrange("p dt s -> p (dt s)"),
                            selc_sb[:, b:b + 1],
                            hin.ap().rearrange("p dt s -> p (dt s)"),
                            ALU.mult, ALU.add)
                        ctx.inc(i, "vv")
                    comb_vv.append(ctx.cnt["vv"])
                hin_done = ctx.cnt["vv"]
                phC.close()

                g_store = {}
                with contextlib.ExitStack() as phs2:
                    est_s = sbuf("m_est2", [128, 2, S], BF16, phs2)
                    bst = sbuf("m_bst2", [128, 1, S], BF16, phs2)
                    hst = sbuf("m_hst", [128, 1, S], BF16, phs2)
                    trash = sbuf("m_trash2", [128, S], BF16, phs2)
                    yacc = sbuf("m_yacc", [128, 2, S], F32, phs2)
                    gst = sbuf("m_gst", [128, 1, S], BF16, phs2)
                    zxld = sbuf("m_zxld", [128, 2, S], BF16, phs2)
                    for k in range(2):
                        ctx.new_sem(f"s2{k}")
                        ctx.new_sem(f"ys{k}")
                        ctx.new_sem(f"gs{k}")
                    y_st = {}

                    for sh in range(2):
                        reps2 = build_reps(sh, True)
                        s2_vv = {}
                        for dt in range(32):
                            sl = dt % 2

                            @block.sync
                            def _(sy, dt=dt, sl=sl, sh=sh, s2_vv=s2_vv):
                                ctx.wait(sy, "vv", hin_done)
                                if dt == 0:
                                    ctx.wait(sy, reps2[0], reps2[1])
                                if dt >= 2:
                                    ctx.wait(sy, "vv", s2_vv[(dt - 2, "end")])
                                if sh == 0:
                                    if dt < 2:
                                        for zz in dx_store[-2:]:
                                            ctx.wait(sy, zz[0], zz[1])
                                    ctx.inc(sy.dma_start(
                                        out=zxld[:, sl, :],
                                        in_=dxc_dram[dt * 128:(dt + 1) * 128, :]),
                                        f"s2{sl}", 16)
                                else:
                                    ctx.wait(sy, y_st[dt][0], y_st[dt][1])
                                    if dt < 2:
                                        for zz in zstore[-2:]:
                                            ctx.wait(sy, zz[0], zz[1])
                                    ctx.inc(sy.dma_start(
                                        out=yacc[:, sl, :],
                                        in_=y_dram[dt * 128:(dt + 1) * 128, :]),
                                        f"s2{sl}", 16)
                                    ctx.inc(sy.dma_start(
                                        out=zxld[:, sl, :],
                                        in_=sz_dram[dt * 128:(dt + 1) * 128, :]),
                                        f"s2{sl}", 16)
                            ld2 = (f"s2{sl}", ctx.cnt[f"s2{sl}"])

                            if sh == 0:
                                @block.vector
                                def _(v, dt=dt, sl=sl, ld2=ld2):
                                    ctx.wait(v, ld2[0], ld2[1])
                                    if dt >= 2:
                                        ctx.wait(v, y_st[dt - 2][0],
                                                 y_st[dt - 2][1])
                                    ctx.inc(v.tensor_copy(yacc[:, sl, :],
                                                          zxld[:, sl, :]),
                                            "vv")
                            else:
                                @block.vector
                                def _(v, ld2=ld2, sl=sl):
                                    ctx.wait(v, ld2[0], ld2[1])
                                    ctx.inc(v.tensor_copy(trash[:, 0:4],
                                                          zxld[:, sl, 0:4]),
                                            "vv")
                            init_vv = ctx.cnt["vv"]

                            last_ss = {}
                            for si in range(8):
                                gsx = dt * 8 + si
                                s_i = sh * 8 + si

                                @block.scalar
                                def _(s, dt=dt, si=si, gsx=gsx, s_i=s_i,
                                      sh=sh, s2_vv=s2_vv):
                                    if gsx >= 2:
                                        pd, psi = divmod(gsx - 2, 8)
                                        ctx.wait(s, "vv", s2_vv[(pd, psi)])
                                    else:
                                        ctx.wait(s, "vv")
                                    ctx.inc(s.activation(
                                        out=est_s[:, gsx % 2, :],
                                        in_=DT[:, dt, :], func=AF.Exp,
                                        scale=a_sb[:, dt * 16 + s_i:
                                                   dt * 16 + s_i + 1]), "ss")
                                expc = ctx.cnt["ss"]

                                @block.vector
                                def _(v, dt=dt, si=si, sl=sl, gsx=gsx,
                                      s_i=s_i, expc=expc, reps2=reps2):
                                    ctx.wait(v, "ss", expc)
                                    if dt == 0 and si == 0:
                                        ctx.wait(v, reps2[0], reps2[1])
                                    v.tensor_tensor(bst[:, 0, :],
                                                    XC[:, dt, :],
                                                    BR[:, si, :], ALU.mult)
                                    v.tensor_tensor_scan(
                                        hst[:, 0, :],
                                        est_s[:, gsx % 2, :],
                                        bst[:, 0, :],
                                        hin.ap()[:, dt, s_i:s_i + 1],
                                        ALU.mult, ALU.add)
                                    v.tensor_tensor(trash[:, :],
                                                    hst[:, 0, :],
                                                    CR[:, si, :], ALU.mult)
                                    ctx.inc(v.tensor_tensor(
                                        yacc[:, sl, :], yacc[:, sl, :],
                                        trash[:, :], ALU.add), "vv")
                                s2_vv[(dt, si)] = ctx.cnt["vv"]

                            if sh == 0:
                                @block.sync
                                def _(sy, dt=dt, sl=sl, s2_vv=s2_vv):
                                    ctx.wait(sy, "vv", s2_vv[(dt, 7)])
                                    ctx.inc(sy.dma_start(
                                        out=y_dram[dt * 128:(dt + 1) * 128, :],
                                        in_=yacc[:, sl, :]), f"ys{sl}", 16)
                                y_st[dt] = (f"ys{sl}", ctx.cnt[f"ys{sl}"])
                                s2_vv[(dt, "end")] = s2_vv[(dt, 7)]
                            else:
                                @block.vector
                                def _(v, dt=dt, sl=sl, s2_vv=s2_vv):
                                    if dt >= 1:
                                        ctx.wait(v, g_store[dt - 1][0],
                                                 g_store[dt - 1][1])
                                    ctx.inc(v.tensor_tensor(
                                        gst[:, 0, :], yacc[:, sl, :],
                                        zxld[:, sl, :], ALU.mult), "vv")
                                s2_vv[(dt, "end")] = ctx.cnt["vv"]

                                @block.sync
                                def _(sy, dt=dt, sl=sl, s2_vv=s2_vv):
                                    ctx.wait(sy, "vv", s2_vv[(dt, "end")])
                                    ctx.inc(sy.dma_start(
                                        out=g_dram[dt * 128:(dt + 1) * 128, :],
                                        in_=gst[:, 0, :]), f"gs{sl}", 16)
                                g_store[dt] = (f"gs{sl}", ctx.cnt[f"gs{sl}"])

            with contextlib.ExitStack() as ph:
                @block.sync
                def _(sy):
                    ctx.wait(sy, "ds")
                    ctx.wait(sy, "gs0")
                    ctx.wait(sy, "gs1")
                gt = sbuf("m_gt", [128, 2, S], BF16, ph)
                wo = sbuf("m_wo", [128, 4, 32 * 128], BF16, ph)
                pso = [psum(f"m_pso{i}", [128, 512], ph) for i in range(8)]
                est = new_evac_state()
                grp_mm = []
                for grp in range(4):
                    w_lds = []
                    for oi in range(4):
                        ot = grp * 4 + oi

                        @block.gpsimd
                        def _(g, ot=ot, oi=oi, grp=grp):
                            if grp == 0 and oi == 0:
                                ctx.wait(g, "ds")
                                ctx.wait(g, "vv")
                            if grp >= 1 and oi == 0:
                                ctx.wait(g, "vv", est["add"][-1])
                            ctx.dinc(g.dma_start(
                                out=wo[:, oi, :],
                                in_=woutp_p[ot * 128:(ot + 1) * 128, :]), "dw")
                        w_lds.append(ctx.cnt["dw"])

                    ct_mm = []
                    for ct in range(32):
                        gsm = ctx.new_sem(f"gt{ct % 2}")

                        @block.sync
                        def _(sy, grp=grp, ct=ct, gsm=gsm):
                            if ct >= 2:
                                ctx.wait(sy, "mm", ct_mm[ct - 2])
                            ctx.inc(sy.dma_start(
                                out=gt[:, ct % 2, :],
                                in_=g_dram[ct * 128:(ct + 1) * 128, :]),
                                gsm, 16)
                        gld = (gsm, ctx.cnt[gsm])

                        @block.tensor
                        def _(t, grp=grp, ct=ct, gld=gld):
                            ctx.wait(t, gld[0], gld[1])
                            ctx.wait(t, "dw", w_lds[3])
                            if grp >= 1 and ct == 0:
                                ctx.wait(t, "vv", est["add"][-1])
                            wview = wo.ap().rearrange(
                                "p sl (ct o) -> p sl ct o", o=128)
                            i = None
                            for oi in range(4):
                                for qc in range(2):
                                    i = t.matmul(
                                        pso[oi * 2 + qc][:],
                                        wview[:, oi, ct, :],
                                        gt[:, ct % 2, qc * 512:(qc + 1) * 512],
                                        start=(ct == 0), stop=(ct == 31))
                            ctx.inc(i, "mm")
                        ct_mm.append(ctx.cnt["mm"])
                    grp_done = ctx.cnt["mm"]

                    for oi in range(4):
                        for qc in range(2):
                            res_evac(est, grp * 4 + oi, qc, pso[oi * 2 + qc],
                                     grp_done, [lx_dram], amid_dram)

            with contextlib.ExitStack() as ph:
                ASB = sbuf("b_a", [128, 32, S], BF16, ph)
                GS = sbuf("b_g", [128, 32, S], BF16, ph)
                phF = ph.enter_context(contextlib.ExitStack())
                lnx2 = sbuf("b_lnx2", [128, 16, S], BF16, phF)
                ln_phase(amid_dram, lnx2, wcol=ln2w_sb, bcol=ln2b_sb, tag="b2")
                lnx2_done = ctx.cnt["vv"]

                ps = [psum(f"b_ps{i}", [128, 512], ph) for i in range(4)]
                ws_f = WStream("b_wsf", phF, wffp_p, 16)
                ev_f = []
                for ot in range(64):
                    @block.gpsimd
                    def _(g, ot=ot):
                        if ot == 0:
                            ctx.wait(g, "vv")
                            ctx.wait(g, "ss")
                        if ot >= 3:
                            sem, cnt = ev_f[2 * (ot - 3) + 1]
                            g.wait_ge(ctx.sem[sem], cnt)
                        ws_f.load(g, ot)
                    ld = ws_f.loaded[ot]

                    for qc in range(2):
                        idx = 2 * ot + qc

                        @block.tensor
                        def _(t, ot=ot, qc=qc, ld=ld, idx=idx):
                            ctx.wait(t, ld[0], ld[1])
                            ctx.wait(t, "vv", lnx2_done)
                            if idx >= 2:
                                sem, cnt = ev_f[idx - 2]
                                ctx.wait(t, sem, cnt)
                            pp = ps[idx % 2]
                            for ct in range(16):
                                i = t.matmul(
                                    pp[:], ws_f.tile(ot, ct),
                                    lnx2[:, ct, qc * 512:(qc + 1) * 512],
                                    start=(ct == 0), stop=(ct == 15))
                            ctx.inc(i, "mm")
                        mmv = ctx.cnt["mm"]

                        if ot < 32:
                            @block.vector
                            def _(v, ot=ot, qc=qc, mmv=mmv, idx=idx):
                                ctx.wait(v, "mm", mmv)
                                ctx.inc(v.tensor_copy(
                                    ASB[:, ot, qc * 512:(qc + 1) * 512],
                                    ps[idx % 2][:]), "vv")
                            ev_f.append(("vv", ctx.cnt["vv"]))
                        else:
                            @block.scalar
                            def _(s, ot=ot, qc=qc, mmv=mmv, idx=idx):
                                ctx.wait(s, "mm", mmv)
                                ctx.inc(s.activation(
                                    out=GS[:, ot - 32,
                                           qc * 512:(qc + 1) * 512],
                                    in_=ps[idx % 2][:], func=AF.Silu), "ss")
                            ev_f.append(("ss", ctx.cnt["ss"]))
                ffp_ss = ctx.cnt["ss"]
                ffp_vv = ctx.cnt["vv"]
                phF.close()

                @block.vector
                def _(v):
                    ctx.wait(v, "vv", ffp_vv)
                    ctx.wait(v, "ss", ffp_ss)
                    ctx.inc(v.tensor_tensor(
                        ASB.ap().rearrange("p dt s -> p (dt s)"),
                        ASB.ap().rearrange("p dt s -> p (dt s)"),
                        GS.ap().rearrange("p dt s -> p (dt s)"),
                        ALU.mult), "vv")
                hmid_done = ctx.cnt["vv"]

                ws_o = WStream("b_wso", ph, wffo_p, 32, slots=2)
                est = new_evac_state()
                for ot in range(16):
                    @block.gpsimd
                    def _(g, ot=ot):
                        rel = (est["add"][2 * (ot - 2) + 1]
                               if ot >= 2 else hmid_done)
                        ws_o.load(g, ot, rel)
                    ld = ws_o.loaded[ot]

                    for qc in range(2):
                        idx = 2 * ot + qc

                        @block.tensor
                        def _(t, ot=ot, qc=qc, ld=ld, idx=idx):
                            ctx.wait(t, ld[0], ld[1])
                            ctx.wait(t, "vv", hmid_done)
                            if idx >= 2:
                                ctx.wait(t, "vv", est["add"][idx - 2])
                            pp = ps[2 + idx % 2]
                            for ct in range(32):
                                i = t.matmul(
                                    pp[:], ws_o.tile(ot, ct),
                                    ASB[:, ct, qc * 512:(qc + 1) * 512],
                                    start=(ct == 0), stop=(ct == 31))
                            ctx.inc(i, "mm")
                        mmv = ctx.cnt["mm"]
                        res_evac(est, ot, qc, ps[2 + idx % 2], mmv,
                                 [xres, amid_dram], xres)

        phase_attn()
        if upto >= 2:
            phase_ffn(2, xres, xres)
        if upto >= 3:
            phase_mamba()
        if upto >= 4:
            phase_ffn(4, xres, y_out)

        if upto < 4:
            cp_v = []
            for ct in range(16):
                @block.sync
                def _(sy, ct=ct):
                    if ct >= 2:
                        ctx.wait(sy, "vv", cp_v[ct - 2])
                    ctx.wait(sy, "ds")
                    ctx.dinc(sy.dma_start(
                        out=xin_st[:, ct % 2, :],
                        in_=xres[ct * 128:(ct + 1) * 128, 0:512]), "ds")
                    ctx.dinc(sy.dma_start(
                        out=xmid_st[:, ct % 2, :],
                        in_=xres[ct * 128:(ct + 1) * 128, 512:1024]), "ds")
                ld = ctx.cnt["ds"]

                @block.vector
                def _(v, ct=ct, ld=ld):
                    ctx.wait(v, "ds", ld)
                    v.tensor_copy(xout_st[:, 0, :], xin_st[:, ct % 2, :])
                    ctx.inc(v.tensor_copy(xout_st[:, 1, :],
                                          xmid_st[:, ct % 2, :]), "vv")
                cp_v.append(ctx.cnt["vv"])

                @block.sync
                def _(sy, ct=ct):
                    ctx.wait(sy, "vv", cp_v[-1])
                    ctx.dinc(sy.dma_start(
                        out=y_out[ct * 128:(ct + 1) * 128, 0:512],
                        in_=xout_st[:, 0, :]), "ds")
                    ctx.dinc(sy.dma_start(
                        out=y_out[ct * 128:(ct + 1) * 128, 512:1024],
                        in_=xout_st[:, 1, :]), "ds")

        @block.gpsimd
        def _(g):
            for nm in list(ctx.sem):
                if nm != "gg":
                    ctx.wait(g, nm)
            ctx.wait(g, "gg", 1)
            srcs = {"kag": kag_out[0:4096, :], "xres": xres[:, :],
                    "ldram": l_dram[:, :], "lx": lx_dram[:, :],
                    "amid": amid_dram[:, :],
                    "g": g_dram[:, :], "sz": sz_dram[:, :],
                    "vag": vag_out.rearrange("(a b) v -> a (b v)", b=2)}
            if dbg is not None and dbg in srcs:
                sap = srcs[dbg]
                rows, cols = sap.shape[0], sap.shape[-1]
                for r0 in range(0, rows, 128):
                    ctx.dinc(g.dma_start(out=xin_st[:, :, :].rearrange(
                        "p a b -> p (a b)")[:, 0:cols], in_=sap[r0:r0 + 128]),
                        "dw")
                    ctx.wait(g, "dw")
                    ctx.dinc(g.dma_start(out=dbg_out[r0:r0 + 128, 0:cols],
                                         in_=xin_st[:, :, :].rearrange(
                        "p a b -> p (a b)")[:, 0:cols]), "dw")
                    ctx.wait(g, "dw")

    return nc



def _prep_w(w, n_ct, n_ot):
    IN, OUT = w.shape
    assert IN == n_ct * 128 and OUT == n_ot * 128, (w.shape, n_ct, n_ot)
    return np.ascontiguousarray(
        w.reshape(n_ct, 128, n_ot, 128).transpose(2, 1, 0, 3)
        .reshape(n_ot * 128, n_ct * 128))


def _hadamard(n):
    h = np.array([[1.0]], dtype=np.float32)
    while h.shape[0] < n:
        h = np.block([[h, h], [h, -h]])
    return np.ascontiguousarray(h.astype(np.float32))


def kernel(**inp):
    f = lambda a: np.ascontiguousarray(np.asarray(a, dtype=np.float32))
    x = f(inp["L0_attn_a_x"])[0]
    xT = np.ascontiguousarray(x.T)

    wq = f(inp["L0_attn_a_w_q"])
    wkv = f(inp["L0_attn_a_w_kv"])
    wdense = f(inp["L0_attn_a_w_dense"])
    in_proj = np.ascontiguousarray(f(inp["L0_attn_b_in_proj_w"]).T)
    x_proj = np.ascontiguousarray(f(inp["L0_attn_b_x_proj_w"]).T)
    dt_proj = np.ascontiguousarray(f(inp["L0_attn_b_dt_proj_w"]).T)
    out_proj = np.ascontiguousarray(f(inp["L0_attn_b_out_proj_w"]).T)
    ff_proj = np.ascontiguousarray(f(inp["L0_attn_b_ff_proj_w"]).T)
    ff_out = np.ascontiguousarray(f(inp["L0_attn_b_ff_out_w"]).T)
    w1 = np.ascontiguousarray(f(inp["L0_ffn_W1"]).T)
    w3 = np.ascontiguousarray(f(inp["L0_ffn_W3"]).T)
    w2t_full = np.ascontiguousarray(f(inp["L0_ffn_W2"]).T)

    xp_pad = np.zeros((DI, 256), np.float32)
    xp_pad[:, :160] = x_proj

    com = {
        "wq_p": _prep_w(wq, 16, 16),
        "wk_p": _prep_w(np.ascontiguousarray(wkv[:, :512]), 16, 4),
        "wv_h": np.ascontiguousarray(wkv[:, 512:]),
        "wdense_p": _prep_w(wdense, 16, 16),
        "winp_p": _prep_w(in_proj, 16, 64),
        "wxp_p": _prep_w(xp_pad, 32, 2),
        "wdtp_p": _prep_w(dt_proj, 1, 32),
        "woutp_p": _prep_w(out_proj, 32, 16),
        "wffp_p": _prep_w(ff_proj, 16, 64),
        "wffo_p": _prep_w(ff_out, 32, 16),
        "w1_p": _prep_w(w1, 16, 64),
        "w3_p": _prep_w(w3, 16, 64),
        "conv_w": np.ascontiguousarray(
            f(inp["L0_attn_b_conv_w"])[:, 0, :].reshape(32, 128, 4)
            .transpose(1, 0, 2).reshape(128, 128)),
        "conv_b": np.ascontiguousarray(
            f(inp["L0_attn_b_conv_b"]).reshape(32, 128).T),
        "dt_b": np.ascontiguousarray(
            f(inp["L0_attn_b_dt_proj_b"]).reshape(32, 128).T),
        "a_log": np.ascontiguousarray(
            f(inp["L0_attn_b_A_log"]).reshape(32, 128, NST)
            .transpose(1, 0, 2).reshape(128, 32 * NST)),
        "d_skip": np.ascontiguousarray(
            f(inp["L0_attn_b_D_skip"]).reshape(32, 128).T),
        "ln1_w": np.ascontiguousarray(
            f(inp["L0_attn_b_ln1_w"]).reshape(16, 128).T),
        "ln1_b": np.ascontiguousarray(
            f(inp["L0_attn_b_ln1_b"]).reshape(16, 128).T),
        "ln2_w": np.ascontiguousarray(
            f(inp["L0_attn_b_ln2_w"]).reshape(16, 128).T),
        "ln2_b": np.ascontiguousarray(
            f(inp["L0_attn_b_ln2_b"]).reshape(16, 128).T),
        "h128": _hadamard(128),
        "kidx": np.ascontiguousarray(
            (np.arange(64)[None, :] * 128
             + np.arange(128)[:, None]).astype(np.float32)),
    }

    in_maps = []
    for c in range(NCORES):
        m = dict(com)
        m["x_slab"] = np.ascontiguousarray(xT[:, c * S:(c + 1) * S])
        shard = w2t_full[:, c * 256:(c + 1) * 256]
        m["w2t"] = np.ascontiguousarray(
            shard.reshape(64, 128, 256).transpose(1, 0, 2).reshape(128, 64 * 256))
        m["qidx"] = np.ascontiguousarray(np.broadcast_to(
            (c * S + np.arange(S)).astype(np.float32), (128, S)))
        se = np.zeros((8, 1), np.float32)
        if c > 0:
            se[c - 1, 0] = 1.0
        m["sel_edge"] = se
        sc = np.zeros((128, 8), np.float32)
        if c > 0:
            sc[:, c - 1] = 1.0
        m["sel_carry"] = sc
        in_maps.append(m)

    nc = build_program(upto=_UPTO, dbg=_DBG)
    res = run_bass_kernel_spmd(nc, in_maps, core_ids=list(range(NCORES)))
    out = np.concatenate(
        [np.asarray(res.results[c]["y"]).T for c in range(NCORES)], axis=0)
    if _DBG is not None:
        kernel.dbg = [np.asarray(res.results[c]["dbg"]) for c in range(NCORES)]
    return out[None, :, :].astype(np.float32)
